# version 02 (final)
# baseline (speedup 1.0000x reference)
import numpy as np
import ml_dtypes
import concourse.bass as bass
import concourse.mybir as mybir
from concourse.bass_utils import run_bass_kernel_spmd

F32 = mybir.dt.float32
BF16 = mybir.dt.bfloat16
AF = mybir.ActivationFunctionType
ALU = mybir.AluOpType
AX = mybir.AxisListType

NCORES = 8
B_FULL, S, E = 32, 2048, 1024
BL = B_FULL // NCORES
TOK = BL * S
L, D, DST, DTR, KC = 4, 2048, 16, 64, 4
NE, H, DH, EH = 8, 16, 64, 512
EPS = 1e-5


class Builder:
    def __init__(self):
        self.phases = []
        self.cur = None
        self.ph()

    def ph(self):
        self.cur = []
        self.phases.append(self.cur)

    def op(self, eng, fn, inc=1):
        self.cur.append((eng, fn, inc))

    def dma(self, eng, out, in_):
        self.cur.append((eng, lambda e, o=out, i=in_: e.dma_start(out=o, in_=i), 16))

    def emit(self, blk, SEM):
        cums = []
        tot = 0
        for p in self.phases:
            cums.append(tot)
            tot += sum(x[2] for x in p)

        def mk(name):
            def inner(e):
                last = -1
                for pi, p in enumerate(self.phases):
                    mine = [x for x in p if x[0] == name]
                    if not mine:
                        continue
                    if cums[pi] > last:
                        e.wait_ge(SEM, cums[pi])
                        last = cums[pi]
                    for _, fn, inc in mine:
                        fn(e).then_inc(SEM, inc)
            return inner

        for en in ["vector", "scalar", "gpsimd", "tensor", "sync"]:
            getattr(blk, en)(mk(en))
        return tot


def build_program():
    nc = bass.Bass("TRN2", target_bir_lowering=False)
    dt = nc.dram_tensor

    x_d = dt("x", [TOK, E], F32, kind="ExternalInput").ap()
    norm_w_d = dt("norm_w", [L, E], F32, kind="ExternalInput").ap()
    win_d = dt("Win", [L, E, 2 * D], F32, kind="ExternalInput").ap()
    convw_d = dt("conv_w", [L, D, KC], F32, kind="ExternalInput").ap()
    convb_d = dt("conv_b", [L, D], F32, kind="ExternalInput").ap()
    wxre_d = dt("Wx_re", [L, D, 128], F32, kind="ExternalInput").ap()
    wdt_d = dt("Wdt", [L, DTR, D], F32, kind="ExternalInput").ap()
    bdt_d = dt("bdt", [L, D], F32, kind="ExternalInput").ap()
    alog_d = dt("A_log", [L, D, DST], F32, kind="ExternalInput").ap()
    dp_d = dt("Dp", [L, D], F32, kind="ExternalInput").ap()
    wblk_d = dt("Wblk", [L, D, E], F32, kind="ExternalInput").ap()
    wout_d = dt("Wout", [E, E], F32, kind="ExternalInput").ap()
    bout_d = dt("bout", [E], F32, kind="ExternalInput").ap()
    rw1t_d = dt("rW1T", [E, EH], F32, kind="ExternalInput").ap()
    rb1_d = dt("rb1", [EH], F32, kind="ExternalInput").ap()
    lng_d = dt("ln_g", [EH], F32, kind="ExternalInput").ap()
    lnb_d = dt("ln_b", [EH], F32, kind="ExternalInput").ap()
    rw2t_d = dt("rW2T", [EH, NE], F32, kind="ExternalInput").ap()
    rb2_d = dt("rb2", [NE], F32, kind="ExternalInput").ap()
    qwt_d = dt("qWT", [NE, E, E], F32, kind="ExternalInput").ap()
    qb_d = dt("qb", [NE, E], F32, kind="ExternalInput").ap()
    kwt_d = dt("kWT", [NE, E, E], F32, kind="ExternalInput").ap()
    kb_d = dt("kb", [NE, E], F32, kind="ExternalInput").ap()
    vwt_d = dt("vWT", [NE, E, E], F32, kind="ExternalInput").ap()
    vb_d = dt("vb", [NE, E], F32, kind="ExternalInput").ap()
    owt_d = dt("oWT", [NE, E, E], F32, kind="ExternalInput").ap()
    ob_d = dt("ob", [NE, E], F32, kind="ExternalInput").ap()
    eye16_d = dt("eye16", [128, 128], BF16, kind="ExternalInput").ap()
    sel_d = dt("sel", [16, 2048], F32, kind="ExternalInput").ap()

    out_d = dt("out", [TOK, E], F32, kind="ExternalOutput").ap()

    r1_d = dt("r1", [TOK, E], F32, kind="Internal").ap()
    r2_d = dt("r2", [TOK, E], F32, kind="Internal").ap()
    h_d = dt("hres", [TOK, E], F32, kind="Internal").ap()
    xcs_d = dt("xcs", [16, 128, D], F32, kind="Internal").ap()
    zs_d = dt("zs", [16, 128, D], F32, kind="Internal").ap()
    ygz_d = dt("ygz", [16, 128, D], BF16, kind="Internal").ap()
    qwtb_d = dt("qwtb", [NE, 8, 128, 1024], BF16, kind="Internal").ap()
    kwtb_d = dt("kwtb", [NE, 8, 128, 1024], BF16, kind="Internal").ap()
    vwtb_d = dt("vwtb", [NE, 128, 8 * 1024], BF16, kind="Internal").ap()
    owtb_d = dt("owtb", [NE, 128, 8 * 1024], BF16, kind="Internal").ap()
    winb_d = dt("winb", [L, 8, 128, 2 * D], BF16, kind="Internal").ap()
    wblkb_d = dt("wblkb", [L, 16, 128, E], BF16, kind="Internal").ap()
    wdtb_d = dt("wdtb", [L, 64, D], BF16, kind="Internal").ap()
    wxreb_d = dt("wxreb", [L, D, 128], BF16, kind="Internal").ap()
    woutb_d = dt("woutb", [8, 128, E], BF16, kind="Internal").ap()
    rw1tb_d = dt("rw1tb", [8, 128, EH], BF16, kind="Internal").ap()
    rw2tb_d = dt("rw2tb", [4, 128, NE], BF16, kind="Internal").ap()
    obb_d = dt("obb", [8, 1024], BF16, kind="Internal").ap()

    from contextlib import ExitStack
    es = ExitStack()

    def sbt(name, shape, dtyp):
        return es.enter_context(nc.sbuf_tensor(name, shape, dtyp))

    y_T = sbt("y_T", [128, 16384], BF16)
    voW = sbt("voW", [128, 8192], BF16)
    vhat = sbt("vhat", [128, 16640], BF16)
    P_T = sbt("P_T", [128, 8192], BF16)
    o_T = sbt("o_T", [128, 16384], BF16)
    qkT = sbt("qkT", [128, 4096], BF16)
    lwb = sbt("lwb", [128, 1024], BF16)
    eye_b = sbt("eye_b", [128, 128], BF16)
    fA = sbt("fA", [128, 2048], F32)
    fB = sbt("fB", [128, 2048], F32)
    fC = sbt("fC", [128, 2048], F32)
    fE = sbt("fE", [128, 2048], F32)
    fG = sbt("fG", [128, 2048], F32)
    dbca = sbt("dbca", [128, 2048], F32)
    consts = sbt("consts", [128, 2048], F32)
    onesrow = sbt("onesrow", [1, 768], F32)
    bias_bc = sbt("bias_bc", [128, 1024], F32)
    smalls = sbt("smalls", [128, 64], F32)
    gates = sbt("gates", [128, 128], F32)
    mam = sbt("mam", [128, 40], F32)
    vqkb = sbt("vqkb", [128, 32], F32)

    ps2a = es.enter_context(nc.psum_tensor("ps2a", [128, 2048], F32))
    ps2b = es.enter_context(nc.psum_tensor("ps2b", [128, 1024], F32))
    pstr = es.enter_context(nc.psum_tensor("pstr", [128, 1024], BF16))

    SEM = es.enter_context(nc.semaphore("SEM"))

    bld = Builder()
    P, O, DMA = bld.ph, bld.op, bld.dma

    ONES64 = 0
    ONES128 = 64
    ones1x128 = onesrow[0:1, ONES128:ONES128 + 128]
    ones1x64 = onesrow[0:1, ONES64:ONES64 + 64]

    P()
    DMA("sync", consts[0:16, :], sel_d[:, :])
    DMA("sync", consts[32:48, :], sel_d[:, :])
    DMA("sync", eye_b[:, :], eye16_d[:, :])
    DMA("sync", onesrow[0:1, 192:704], rb1_d.rearrange("(o j) -> o j", o=1))
    DMA("sync", onesrow[0:1, 704:712], rb2_d.rearrange("(o j) -> o j", o=1))
    DMA("sync", fB[0:1, 0:512], lng_d.rearrange("(o j) -> o j", o=1))
    DMA("sync", fB[0:1, 512:1024], lnb_d.rearrange("(o j) -> o j", o=1))
    P()
    O("vector", lambda e: e.memset(onesrow[0:1, 0:192], 1.0))
    P()
    O("tensor", lambda e: e.matmul(ps2a[:, 0:512], ones1x128, fB[0:1, 0:512], start=True, stop=True))
    O("tensor", lambda e: e.matmul(ps2a[:, 512:1024], ones1x128, fB[0:1, 512:1024], start=True, stop=True))
    P()
    O("vector", lambda e: e.tensor_copy(out=bias_bc[:, :], in_=ps2a[:, 0:1024]))
    P()
    LNG, LNB = 0, 512
    rb1row = onesrow[0:1, 192:704]
    rb2row = onesrow[0:1, 704:712]

    def conv_store(cols, dst):
        P()
        O("vector", lambda e: e.tensor_copy(out=qkT[:, 0:cols], in_=fG[:, 0:cols]))
        P()
        DMA("sync", dst, qkT[:, 0:cols])

    for mat_d, matb_d in ((qwt_d, qwtb_d), (kwt_d, kwtb_d)):
        qscale = 0.125 if mat_d is qwt_d else 1.0
        for ei in range(NE):
            for ec in range(8):
                P()
                DMA("sync", fG[:, 0:1024], mat_d[ei, ec * 128:(ec + 1) * 128, :])
                P()
                O("vector", lambda e, qs=qscale: e.tensor_scalar_mul(qkT[:, 0:1024], fG[:, 0:1024], qs))
                P()
                DMA("sync",
                    matb_d[ei, :, :, ec * 128:ec * 128 + 128].rearrange("hp p i -> p hp i"),
                    qkT[:, 0:1024].rearrange("p (hp i) -> p hp i", hp=8))
    for mat_d, matb_d in ((vwt_d, vwtb_d), (owt_d, owtb_d)):
        for ei in range(NE):
            for ec in range(8):
                P()
                DMA("sync", fG[:, 0:1024], mat_d[ei, ec * 128:(ec + 1) * 128, :])
                conv_store(1024, matb_d[ei, :, ec * 1024:(ec + 1) * 1024])
    for l in range(L):
        P()
        for ec in range(8):
            DMA("sync", smalls[:, ec:ec + 1],
                norm_w_d[l, ec * 128:(ec + 1) * 128].rearrange("(p o) -> p o", o=1))
        for ec in range(8):
            for hf in range(2):
                P()
                DMA("sync", fG[:, 0:2048], win_d[l, ec * 128:(ec + 1) * 128, hf * 2048:(hf + 1) * 2048])
                P()
                O("vector", lambda e, ec=ec: e.tensor_scalar_mul(qkT[:, 0:2048], fG[:, 0:2048], smalls[:, ec:ec + 1]))
                P()
                DMA("sync", winb_d[l, ec, :, hf * 2048:(hf + 1) * 2048], qkT[:, 0:2048])
    for l in range(L):
        for dc in range(16):
            P()
            DMA("sync", fG[:, 0:1024], wblk_d[l, dc * 128:(dc + 1) * 128, :])
            conv_store(1024, wblkb_d[l, dc, :, :])
    for l in range(L):
        P()
        DMA("sync", fG[0:64, 0:2048], wdt_d[l, :, :])
        P()
        O("vector", lambda e: e.tensor_copy(out=qkT[0:64, 0:2048], in_=fG[0:64, 0:2048]))
        P()
        DMA("sync", wdtb_d[l, :, :], qkT[0:64, 0:2048])
    for l in range(L):
        for dc in range(16):
            P()
            DMA("sync", fG[:, 0:128], wxre_d[l, dc * 128:(dc + 1) * 128, :])
            conv_store(128, wxreb_d[l, dc * 128:(dc + 1) * 128, :])
    for ec in range(8):
        P()
        DMA("sync", fG[:, 0:1024], wout_d[ec * 128:(ec + 1) * 128, :])
        conv_store(1024, woutb_d[ec, :, :])
    for ec in range(8):
        P()
        DMA("sync", fG[:, 0:512], rw1t_d[ec * 128:(ec + 1) * 128, :])
        conv_store(512, rw1tb_d[ec, :, :])
    for jc in range(4):
        P()
        DMA("sync", fG[:, 0:8], rw2t_d[jc * 128:(jc + 1) * 128, :])
        conv_store(8, rw2tb_d[jc, :, :])
    P()
    DMA("sync", fG[0:8, 0:1024], ob_d[:, :])
    P()
    O("vector", lambda e: e.tensor_copy(out=qkT[0:8, 0:1024], in_=fG[0:8, 0:1024]))
    P()
    DMA("sync", obb_d[:, :], qkT[0:8, 0:1024])
    P()

    def layernorm_tile(xin, out_ap):
        O("vector", lambda e: e.reduce_sum(out=smalls[:, 0:1], in_=xin, axis=AX.X))
        O("scalar", lambda e: e.activation(out=fE[:, 0:1024], in_=xin, func=AF.Square,
                                           accum_out=smalls[:, 1:2]))
        P()
        O("vector", lambda e: e.tensor_scalar_mul(smalls[:, 2:3], smalls[:, 0:1], -1.0 / E))
        P()
        O("vector", lambda e: e.tensor_scalar(out=smalls[:, 3:4], in0=smalls[:, 2:3],
                                              scalar1=smalls[:, 2:3], scalar2=None, op0=ALU.mult))
        P()
        O("vector", lambda e: e.tensor_scalar(out=smalls[:, 4:5], in0=smalls[:, 1:2],
                                              scalar1=1.0 / E, scalar2=smalls[:, 3:4],
                                              op0=ALU.mult, op1=ALU.subtract))
        P()
        O("vector", lambda e: e.tensor_scalar_add(smalls[:, 4:5], smalls[:, 4:5], float(EPS)))
        P()
        O("scalar", lambda e: e.activation(out=smalls[:, 5:6], in_=smalls[:, 4:5],
                                           func=AF.Sqrt))
        P()
        O("vector", lambda e: e.reciprocal(out=smalls[:, 6:7], in_=smalls[:, 5:6]))
        P()
        O("vector", lambda e: e.tensor_scalar(out=out_ap, in0=xin,
                                              scalar1=smalls[:, 2:3], scalar2=smalls[:, 6:7],
                                              op0=ALU.add, op1=ALU.mult))
        P()

    def transpose_8(in_bf, out_strided):
        for ec in range(8):
            O("tensor", lambda e, ec=ec: e.transpose(pstr[:, ec * 128:(ec + 1) * 128],
                                                     in_bf[:, ec * 128:(ec + 1) * 128], eye_b[:, :]))
        P()
        O("vector", lambda e: e.tensor_copy(
            out=out_strided, in_=pstr[:, :].rearrange("p (ec t) -> p ec t", ec=8)))
        P()

    h_T = o_T

    def ffn_stage(in_resid, out_resid):
        for b in range(BL):
            tok0 = b * S
            P()
            for ec in range(8):
                DMA("sync", voW[:, ec * 512:(ec + 1) * 512], rw1tb_d[ec, :, :])
            for jc in range(4):
                DMA("sync", voW[:, 4096 + jc * 8:4096 + (jc + 1) * 8], rw2tb_d[jc, :, :])
            DMA("sync", vhat[0:8, 0:1024], obb_d[:, :])
            P()
            for tc in range(16):
                r0 = tok0 + tc * 128
                P()
                DMA("sync", fA[:, 0:1024], in_resid[r0:r0 + 128, :])
                P()
                layernorm_tile(fA[:, 0:1024], fB[:, 0:1024])
                O("vector", lambda e: e.tensor_copy(out=lwb[:, 0:1024], in_=fB[:, 0:1024]))
                P()
                transpose_8(lwb[:, 0:1024],
                            y_T.rearrange("p (ec t) -> p ec t", ec=8)[:, :, tc * 128:tc * 128 + 128])
                for ec in range(8):
                    O("tensor", lambda e, ec=ec, tc=tc: e.matmul(
                        ps2b[:, 0:512],
                        y_T[:, ec * 2048 + tc * 128: ec * 2048 + tc * 128 + 128],
                        voW[:, ec * 512:(ec + 1) * 512],
                        start=(ec == 0), stop=False, skip_group_check=True))
                O("tensor", lambda e: e.matmul(ps2b[:, 0:512], ones1x128, rb1row,
                                               start=False, stop=True, skip_group_check=True))
                P()
                O("vector", lambda e: e.tensor_copy(out=fC[:, 0:512], in_=ps2b[:, 0:512]))
                P()
                O("vector", lambda e: e.reduce_sum(out=smalls[:, 8:9], in_=fC[:, 0:512], axis=AX.X))
                O("scalar", lambda e: e.activation(out=fE[:, 0:512], in_=fC[:, 0:512], func=AF.Square,
                                                   accum_out=smalls[:, 9:10]))
                P()
                O("vector", lambda e: e.tensor_scalar_mul(smalls[:, 10:11], smalls[:, 8:9], -1.0 / EH))
                P()
                O("vector", lambda e: e.tensor_scalar(out=smalls[:, 11:12], in0=smalls[:, 10:11],
                                                      scalar1=smalls[:, 10:11], scalar2=None, op0=ALU.mult))
                P()
                O("vector", lambda e: e.tensor_scalar(out=smalls[:, 12:13], in0=smalls[:, 9:10],
                                                      scalar1=1.0 / EH, scalar2=smalls[:, 11:12],
                                                      op0=ALU.mult, op1=ALU.subtract))
                P()
                O("vector", lambda e: e.tensor_scalar_add(smalls[:, 12:13], smalls[:, 12:13], float(EPS)))
                P()
                O("scalar", lambda e: e.activation(out=smalls[:, 13:14], in_=smalls[:, 12:13],
                                                   func=AF.Sqrt))
                P()
                O("vector", lambda e: e.reciprocal(out=smalls[:, 14:15], in_=smalls[:, 13:14]))
                P()
                O("vector", lambda e: e.tensor_scalar(out=fC[:, 0:512], in0=fC[:, 0:512],
                                                      scalar1=smalls[:, 10:11], scalar2=smalls[:, 14:15],
                                                      op0=ALU.add, op1=ALU.mult))
                P()
                O("vector", lambda e: e.tensor_tensor(out=fC[:, 0:512], in0=fC[:, 0:512],
                                                      in1=bias_bc[:, LNG:LNG + 512], op=ALU.mult))
                P()
                O("vector", lambda e: e.tensor_tensor(out=fC[:, 0:512], in0=fC[:, 0:512],
                                                      in1=bias_bc[:, LNB:LNB + 512], op=ALU.add))
                P()
                O("vector", lambda e: e.tensor_scalar_max(lwb[:, 0:512], fC[:, 0:512], 0.0))
                P()
                for jc in range(4):
                    O("tensor", lambda e, jc=jc: e.transpose(pstr[:, jc * 128:(jc + 1) * 128],
                                                             lwb[:, jc * 128:(jc + 1) * 128], eye_b[:, :]))
                P()
                O("vector", lambda e, tc=tc: e.tensor_copy(
                    out=h_T.rearrange("p (jc t) -> p jc t", jc=8)[:, 0:4, tc * 128:tc * 128 + 128],
                    in_=pstr[:, 0:512].rearrange("p (jc t) -> p jc t", jc=4)))
                P()
                for jc in range(4):
                    O("tensor", lambda e, jc=jc, tc=tc: e.matmul(
                        ps2b[:, 512:520],
                        h_T[:, jc * 2048 + tc * 128: jc * 2048 + tc * 128 + 128],
                        voW[:, 4096 + jc * 8:4096 + (jc + 1) * 8],
                        start=(jc == 0), stop=False, skip_group_check=True))
                O("tensor", lambda e: e.matmul(ps2b[:, 512:520], ones1x128, rb2row,
                                               start=False, stop=True, skip_group_check=True))
                P()
                O("vector", lambda e: e.tensor_copy(out=fC[:, 1024:1032], in_=ps2b[:, 512:520]))
                P()
                O("vector", lambda e: e.reduce_max(out=smalls[:, 16:17], in_=fC[:, 1024:1032], axis=AX.X, negate=True))
                P()
                O("vector", lambda e: e.tensor_scalar_add(fC[:, 1032:1040], fC[:, 1024:1032], smalls[:, 16:17]))
                P()
                O("scalar", lambda e: e.activation(out=fC[:, 1040:1048], in_=fC[:, 1032:1040], func=AF.Exp))
                P()
                O("vector", lambda e: e.reduce_max(out=smalls[:, 17:18], in_=fC[:, 1040:1048], axis=AX.X))
                P()
                O("vector", lambda e: e.tensor_scalar(out=fC[:, 1048:1056], in0=fC[:, 1040:1048],
                                                      scalar1=smalls[:, 17:18], scalar2=-1e30,
                                                      op0=ALU.is_ge, op1=ALU.mult))
                P()
                O("vector", lambda e: e.tensor_tensor(out=fC[:, 1056:1064], in0=fC[:, 1040:1048],
                                                      in1=fC[:, 1048:1056], op=ALU.add))
                P()
                O("vector", lambda e: e.reduce_max(out=smalls[:, 18:19], in_=fC[:, 1056:1064], axis=AX.X))
                P()
                O("vector", lambda e: e.tensor_scalar(out=fC[:, 1064:1072], in0=fC[:, 1040:1048],
                                                      scalar1=smalls[:, 18:19], scalar2=None, op0=ALU.is_ge))
                P()
                O("vector", lambda e: e.tensor_tensor(out=fC[:, 1072:1080], in0=fC[:, 1040:1048],
                                                      in1=fC[:, 1064:1072], op=ALU.mult))
                P()
                O("vector", lambda e: e.reduce_sum(out=smalls[:, 19:20], in_=fC[:, 1072:1080], axis=AX.X))
                P()
                O("vector", lambda e: e.reciprocal(out=smalls[:, 20:21], in_=smalls[:, 19:20]))
                P()
                O("vector", lambda e: e.tensor_scalar_mul(smalls[:, 21:22], smalls[:, 20:21], 0.5))
                P()
                O("vector", lambda e, tc=tc: e.tensor_scalar_mul(
                    gates[:, tc * 8:(tc + 1) * 8], fC[:, 1072:1080], smalls[:, 21:22]))
                P()
                O("vector", lambda e, tc=tc: e.tensor_copy(out=lwb[:, 256:264], in_=gates[:, tc * 8:(tc + 1) * 8]))
                P()
                O("tensor", lambda e: e.transpose(pstr[0:8, 0:128], lwb[:, 256:264], eye_b[:, :]))
                P()
                O("vector", lambda e: e.tensor_copy(out=lwb[0:8, 0:128], in_=pstr[0:8, 0:128]))
                P()
                O("tensor", lambda e: e.matmul(ps2b[:, 0:512], lwb[0:8, 0:128], vhat[0:8, 0:512], start=True, stop=True))
                O("tensor", lambda e: e.matmul(ps2b[:, 512:1024], lwb[0:8, 0:128], vhat[0:8, 512:1024], start=True, stop=True))
                P()
                O("vector", lambda e: e.tensor_tensor(out=fE[:, 0:1024], in0=ps2b[:, 0:1024], in1=fA[:, 0:1024], op=ALU.add))
                P()
                DMA("gpsimd", out_resid[r0:r0 + 128, :], fE[:, 0:1024])
                P()
            for ei in range(NE):
                P()
                for ec in range(8):
                    DMA("sync", voW[:, ec * 1024:(ec + 1) * 1024], vwtb_d[ei, :, ec * 1024:(ec + 1) * 1024])
                for hp in range(8):
                    DMA("sync", vqkb[:, hp:hp + 1], vb_d[ei, hp * 128:(hp + 1) * 128].rearrange("(p o) -> p o", o=1))
                    DMA("sync", vqkb[:, 8 + hp:9 + hp], qb_d[ei, hp * 128:(hp + 1) * 128].rearrange("(p o) -> p o", o=1))
                    DMA("sync", vqkb[:, 16 + hp:17 + hp], kb_d[ei, hp * 128:(hp + 1) * 128].rearrange("(p o) -> p o", o=1))
                P()
                O("vector", lambda e: e.tensor_scalar_mul(vqkb[:, 8:16], vqkb[:, 8:16], 0.125))
                P()
                for tc in range(16):
                    for nb in range(2):
                        for ec in range(8):
                            O("tensor", lambda e, tc=tc, nb=nb, ec=ec: e.matmul(
                                ps2a[:, nb * 512:(nb + 1) * 512],
                                y_T[:, ec * 2048 + tc * 128: ec * 2048 + tc * 128 + 128],
                                voW[:, ec * 1024 + nb * 512: ec * 1024 + (nb + 1) * 512],
                                start=(ec == 0), stop=(ec == 7)))
                    P()
                    O("vector", lambda e, tc=tc: e.memset(vhat[:, tc * 1040:(tc + 1) * 1040], 1.0))
                    P()
                    O("vector", lambda e, tc=tc: e.tensor_copy(
                        out=vhat[:, tc * 1040:(tc + 1) * 1040].rearrange("p (h c) -> p h c", h=16)[:, :, 0:64],
                        in_=ps2a[:, 0:1024].rearrange("p (h c) -> p h c", h=16)))
                    P()
                P()
                for ec in range(8):
                    DMA("sync", voW[:, ec * 1024:(ec + 1) * 1024], owtb_d[ei, :, ec * 1024:(ec + 1) * 1024])
                P()
                for hp in range(8):
                    P()
                    DMA("sync", lwb[:, 0:1024], qwtb_d[ei, hp, :, :])
                    DMA("sync", P_T[:, 0:1024], kwtb_d[ei, hp, :, :])
                    P()
                    for which in range(2):
                        for nb in range(4):
                            for ec in range(8):
                                src_w = lwb if which == 0 else P_T
                                O("tensor", lambda e, w=src_w, ec=ec, nb=nb: e.matmul(
                                    ps2a[:, nb * 512:(nb + 1) * 512],
                                    w[:, ec * 128: ec * 128 + 128],
                                    y_T[:, ec * 2048 + nb * 512: ec * 2048 + (nb + 1) * 512],
                                    start=(ec == 0), stop=(ec == 7)))
                        P()
                        O("vector", lambda e, which=which, hp=hp: e.tensor_scalar_add(
                            qkT[:, which * 2048:(which + 1) * 2048], ps2a[:, 0:2048],
                            vqkb[:, 8 + which * 8 + hp: 9 + which * 8 + hp]))
                        P()
                    for hh in range(2):
                        h_lo = hh * 64
                        h_abs = hp * 2 + hh
                        for tqb in range(4):
                            for j in range(10):
                                P()
                                for k in (2 * j, 2 * j + 1):
                                    if k < 16:
                                        buf = (k % 4) * 512
                                        O("tensor", lambda e, k=k, buf=buf, h_lo=h_lo, tqb=tqb: e.matmul(
                                            ps2a[:, buf:buf + 512],
                                            qkT[h_lo:h_lo + 64, 2048 + k * 128: 2048 + k * 128 + 128],
                                            qkT[h_lo:h_lo + 64, tqb * 512:(tqb + 1) * 512],
                                            start=True, stop=True, skip_group_check=True))
                                for kk in (2 * j - 2, 2 * j - 1):
                                    if 0 <= kk <= 15:
                                        buf = (kk % 4) * 512
                                        O("scalar", lambda e, kk=kk, buf=buf: e.activation(
                                            out=P_T[:, kk * 512:(kk + 1) * 512], in_=ps2a[:, buf:buf + 512],
                                            func=AF.Exp))
                                for kk in (2 * j - 4, 2 * j - 3):
                                    if 0 <= kk <= 15:
                                        O("tensor", lambda e, kk=kk, h_abs=h_abs: e.matmul(
                                            ps2b[0:65, 0:512],
                                            vhat[:, kk * 1040 + h_abs * 65: kk * 1040 + h_abs * 65 + 65],
                                            P_T[:, kk * 512:(kk + 1) * 512],
                                            start=(kk == 0), stop=(kk == 15), skip_group_check=True))
                            P()
                            O("vector", lambda e: e.reciprocal(out=fB[0:1, 0:512], in_=ps2b[64:65, 0:512]))
                            P()
                            O("tensor", lambda e: e.matmul(ps2a[64:128, 1024:1536], ones1x64, fB[0:1, 0:512], start=True, stop=True))
                            P()
                            O("vector", lambda e: e.tensor_copy(out=dbca[0:64, 0:512], in_=ps2a[64:128, 1024:1536]))
                            P()
                            O("vector", lambda e: e.tensor_tensor(out=dbca[0:64, 512:1024], in0=ps2b[0:64, 0:512],
                                                                  in1=dbca[0:64, 0:512], op=ALU.mult))
                            P()
                            O("vector", lambda e, hh=hh, hp=hp, tqb=tqb: e.tensor_scalar_add(
                                o_T[hh * 64:hh * 64 + 64, hp * 2048 + tqb * 512: hp * 2048 + (tqb + 1) * 512],
                                dbca[0:64, 512:1024],
                                vqkb[hh * 64:hh * 64 + 64, hp:hp + 1]))
                            P()
                for tc in range(16):
                    r0 = tok0 + tc * 128
                    tqb = tc // 4
                    tloc = (tc % 4) * 128
                    P()
                    DMA("sync", fE[:, 0:1024], out_resid[r0:r0 + 128, :])
                    P()
                    for nb in range(2):
                        for ic in range(8):
                            O("tensor", lambda e, nb=nb, ic=ic, tqb=tqb, tloc=tloc: e.matmul(
                                ps2a[:, nb * 512:(nb + 1) * 512],
                                o_T[:, ic * 2048 + tqb * 512 + tloc: ic * 2048 + tqb * 512 + tloc + 128],
                                voW[:, ic * 1024 + nb * 512: ic * 1024 + (nb + 1) * 512],
                                start=(ic == 0), stop=(ic == 7)))
                    P()
                    O("vector", lambda e, tc=tc, ei=ei: e.scalar_tensor_tensor(
                        out=fE[:, 0:1024], in0=ps2a[:, 0:1024],
                        scalar=gates[:, tc * 8 + ei: tc * 8 + ei + 1],
                        in1=fE[:, 0:1024], op0=ALU.mult, op1=ALU.add))
                    P()
                    DMA("gpsimd", out_resid[r0:r0 + 128, :], fE[:, 0:1024])
                    P()

    winh = y_T
    xn_T = o_T
    wblk_sb = vhat
    xcbf = qkT
    dtb_lo = 2048

    def attn_stage(in_resid, out_resid):
        for b in range(BL):
            tok0 = b * S
            for tc in range(16):
                r0 = tok0 + tc * 128
                P()
                DMA("sync", fA[:, 0:1024], in_resid[r0:r0 + 128, :])
                P()
                layernorm_tile(fA[:, 0:1024], fB[:, 0:1024])
                DMA("gpsimd", h_d[r0:r0 + 128, :], fB[:, 0:1024])
                P()
            for l in range(L):
                for tc in range(16):
                    r0 = tok0 + tc * 128
                    P()
                    DMA("sync", fA[:, 0:1024], h_d[r0:r0 + 128, :])
                    P()
                    O("scalar", lambda e: e.activation(out=fB[:, 0:1024], in_=fA[:, 0:1024],
                                                       func=AF.Square, accum_out=smalls[:, 0:1]))
                    P()
                    O("vector", lambda e: e.tensor_scalar(out=smalls[:, 1:2], in0=smalls[:, 0:1],
                                                          scalar1=1.0 / E, scalar2=float(EPS),
                                                          op0=ALU.mult, op1=ALU.add))
                    P()
                    O("scalar", lambda e: e.activation(out=smalls[:, 1:2], in_=smalls[:, 1:2],
                                                       func=AF.Sqrt))
                    P()
                    O("vector", lambda e: e.reciprocal(out=smalls[:, 2:3], in_=smalls[:, 1:2]))
                    P()
                    O("vector", lambda e: e.tensor_scalar_mul(lwb[:, 0:1024], fA[:, 0:1024], smalls[:, 2:3]))
                    P()
                    transpose_8(lwb[:, 0:1024],
                                xn_T.rearrange("p (ec t) -> p ec t", ec=8)[:, :, tc * 128:tc * 128 + 128])
                O("vector", lambda e: e.memset(dbca[:, :], 0.0))
                P()
                for half in range(2):
                    c0 = D if half == 0 else 0
                    P()
                    for ec in range(8):
                        DMA("sync", winh[:, ec * 2048:(ec + 1) * 2048], winb_d[l, ec, :, c0:c0 + D])
                    P()
                    for dc in range(16):
                        for nb in range(4):
                            for ec in range(8):
                                O("tensor", lambda e, dc=dc, nb=nb, ec=ec: e.matmul(
                                    ps2a[:, nb * 512:(nb + 1) * 512],
                                    winh[:, ec * 2048 + dc * 128: ec * 2048 + dc * 128 + 128],
                                    xn_T[:, ec * 2048 + nb * 512: ec * 2048 + (nb + 1) * 512],
                                    start=(ec == 0), stop=(ec == 7)))
                        P()
                        if half == 0:
                            O("vector", lambda e: e.tensor_copy(out=fE[:, 0:2048], in_=ps2a[:, 0:2048]))
                            P()
                            DMA("gpsimd", zs_d[dc, :, :], fE[:, 0:2048])
                            P()
                        else:
                            O("vector", lambda e: e.tensor_copy(out=fE[:, 0:2048], in_=ps2a[:, 0:2048]))
                            DMA("sync", smalls[:, 32:36], convw_d[l, dc * 128:(dc + 1) * 128, :])
                            DMA("sync", smalls[:, 36:37], convb_d[l, dc * 128:(dc + 1) * 128].rearrange("(p o) -> p o", o=1))
                            P()
                            O("vector", lambda e: e.tensor_scalar_mul(fG[:, 0:2048], fE[:, 0:2048], smalls[:, 35:36]))
                            P()
                            for sh in range(1, 4):
                                O("vector", lambda e, sh=sh: e.scalar_tensor_tensor(
                                    out=fG[:, sh:2048], in0=fE[:, 0:2048 - sh], scalar=smalls[:, 35 - sh:36 - sh],
                                    in1=fG[:, sh:2048], op0=ALU.mult, op1=ALU.add))
                                P()
                            O("vector", lambda e: e.tensor_scalar_add(fG[:, 0:2048], fG[:, 0:2048], smalls[:, 36:37]))
                            P()
                            O("scalar", lambda e: e.activation(out=fC[:, 0:2048], in_=fG[:, 0:2048], func=AF.Sigmoid))
                            P()
                            O("vector", lambda e: e.tensor_tensor(out=fG[:, 0:2048], in0=fG[:, 0:2048], in1=fC[:, 0:2048], op=ALU.mult))
                            P()
                            DMA("gpsimd", xcs_d[dc, :, :], fG[:, 0:2048])
                            O("vector", lambda e: e.tensor_copy(out=xcbf[:, 0:2048], in_=fG[:, 0:2048]))
                            P()
                            DMA("sync", lwb[:, 0:128], wxreb_d[l, dc * 128:(dc + 1) * 128, :])
                            P()
                            for nb in range(2):
                                O("tensor", lambda e, nb=nb: e.matmul(
                                    ps2b[:, nb * 512:(nb + 1) * 512],
                                    lwb[:, 0:128],
                                    xcbf[:, nb * 512:(nb + 1) * 512],
                                    start=True, stop=True))
                            P()
                            O("vector", lambda e: e.tensor_tensor(out=dbca[:, 0:1024], in0=dbca[:, 0:1024],
                                                                  in1=ps2b[:, 0:1024], op=ALU.add))
                            P()
                            for nb in range(2):
                                O("tensor", lambda e, nb=nb: e.matmul(
                                    ps2b[:, nb * 512:(nb + 1) * 512],
                                    lwb[:, 0:128],
                                    xcbf[:, 1024 + nb * 512: 1024 + (nb + 1) * 512],
                                    start=True, stop=True))
                            P()
                            O("vector", lambda e: e.tensor_tensor(out=dbca[:, 1024:2048], in0=dbca[:, 1024:2048],
                                                                  in1=ps2b[:, 0:1024], op=ALU.add))
                            P()
                O("scalar", lambda e: e.activation(out=qkT[0:64, dtb_lo:dtb_lo + 2048], in_=dbca[64:128, 0:2048], func=AF.Copy))
                P()
                for dc in range(16):
                    P()
                    DMA("sync", fG[:, 0:2048], xcs_d[dc, :, :])
                    DMA("sync", mam[:, 0:16], alog_d[l, dc * 128:(dc + 1) * 128, :])
                    DMA("sync", smalls[:, 40:41], bdt_d[l, dc * 128:(dc + 1) * 128].rearrange("(p o) -> p o", o=1))
                    DMA("sync", smalls[:, 41:42], dp_d[l, dc * 128:(dc + 1) * 128].rearrange("(p o) -> p o", o=1))
                    DMA("sync", qkT[0:64, 0:2048], wdtb_d[l, :, :])
                    P()
                    O("scalar", lambda e: e.activation(out=mam[:, 16:32], in_=mam[:, 0:16], func=AF.Exp))
                    P()
                    O("vector", lambda e: e.tensor_scalar_mul(mam[:, 16:32], mam[:, 16:32], -1.0))
                    P()
                    for nb in range(4):
                        O("tensor", lambda e, nb=nb, dc=dc: e.matmul(
                            ps2a[:, nb * 512:(nb + 1) * 512],
                            qkT[0:64, dc * 128:dc * 128 + 128],
                            qkT[0:64, dtb_lo + nb * 512: dtb_lo + (nb + 1) * 512],
                            start=True, stop=True))
                    P()
                    O("vector", lambda e: e.tensor_scalar_add(fA[:, 0:2048], ps2a[:, 0:2048], smalls[:, 40:41]))
                    P()
                    O("vector", lambda e: e.tensor_scalar_mul(fB[:, 0:2048], fA[:, 0:2048], -1.0))
                    P()
                    O("vector", lambda e: e.tensor_tensor(out=fB[:, 0:2048], in0=fA[:, 0:2048], in1=fB[:, 0:2048], op=ALU.min))
                    P()
                    O("scalar", lambda e: e.activation(out=fB[:, 0:2048], in_=fB[:, 0:2048], func=AF.Exp))
                    P()
                    O("scalar", lambda e: e.activation(out=fB[:, 0:2048], in_=fB[:, 0:2048], func=AF.Ln, bias=1.0))
                    P()
                    O("vector", lambda e: e.tensor_scalar_max(fC[:, 0:2048], fA[:, 0:2048], 0.0))
                    P()
                    O("vector", lambda e: e.tensor_tensor(out=fA[:, 0:2048], in0=fB[:, 0:2048], in1=fC[:, 0:2048], op=ALU.add))
                    P()
                    O("vector", lambda e: e.tensor_tensor(out=fB[:, 0:2048], in0=fA[:, 0:2048], in1=fG[:, 0:2048], op=ALU.mult))
                    P()
                    O("vector", lambda e: e.tensor_scalar_mul(fE[:, 0:2048], fG[:, 0:2048], smalls[:, 41:42]))
                    P()
                    for n in range(DST):
                        for nb in range(4):
                            O("tensor", lambda e, n=n, nb=nb: e.matmul(
                                ps2a[:, nb * 512:(nb + 1) * 512],
                                consts[0:16, n * 128:n * 128 + 128],
                                dbca[0:16, nb * 512:(nb + 1) * 512],
                                start=True, stop=True))
                        O("scalar", lambda e, n=n: e.activation(out=fC[:, 0:2048], in_=fA[:, 0:2048],
                                                                func=AF.Exp, scale=mam[:, 16 + n:17 + n]))
                        P()
                        O("vector", lambda e: e.tensor_tensor(out=fG[:, 0:2048], in0=fB[:, 0:2048], in1=ps2a[:, 0:2048], op=ALU.mult))
                        P()
                        O("vector", lambda e: e.tensor_tensor_scan(out=fG[:, 0:2048], data0=fC[:, 0:2048],
                                                                   data1=fG[:, 0:2048], initial=0.0,
                                                                   op0=ALU.mult, op1=ALU.add))
                        for nb in range(4):
                            O("tensor", lambda e, n=n, nb=nb: e.matmul(
                                ps2a[:, nb * 512:(nb + 1) * 512],
                                consts[32:48, n * 128:n * 128 + 128],
                                dbca[32:48, nb * 512:(nb + 1) * 512],
                                start=True, stop=True))
                        P()
                        O("vector", lambda e: e.tensor_tensor(out=fC[:, 0:2048], in0=fG[:, 0:2048], in1=ps2a[:, 0:2048], op=ALU.mult))
                        P()
                        O("gpsimd", lambda e: e.tensor_add(out=fE[:, 0:2048], in0=fE[:, 0:2048], in1=fC[:, 0:2048]))
                        P()
                    DMA("sync", fG[:, 0:2048], zs_d[dc, :, :])
                    P()
                    O("scalar", lambda e: e.activation(out=fC[:, 0:2048], in_=fG[:, 0:2048], func=AF.Silu))
                    P()
                    O("vector", lambda e: e.tensor_tensor(out=fE[:, 0:2048], in0=fE[:, 0:2048], in1=fC[:, 0:2048], op=ALU.mult))
                    P()
                    O("vector", lambda e: e.tensor_copy(out=xcbf[:, 0:2048], in_=fE[:, 0:2048]))
                    P()
                    DMA("gpsimd", ygz_d[dc, :, :], xcbf[:, 0:2048])
                    P()
                P()
                for dc in range(16):
                    DMA("sync", wblk_sb[:, dc * 1024:(dc + 1) * 1024], wblkb_d[l, dc, :, :])
                P()
                for tc in range(16):
                    r0 = tok0 + tc * 128
                    P()
                    DMA("sync", fA[:, 0:1024], h_d[r0:r0 + 128, :])
                    for dc in range(16):
                        DMA("sync", xcbf[:, dc * 128:(dc + 1) * 128], ygz_d[dc, :, tc * 128:tc * 128 + 128])
                    P()
                    for nb in range(2):
                        for dc in range(16):
                            O("tensor", lambda e, nb=nb, dc=dc: e.matmul(
                                ps2a[:, nb * 512:(nb + 1) * 512],
                                xcbf[:, dc * 128:(dc + 1) * 128],
                                wblk_sb[:, dc * 1024 + nb * 512: dc * 1024 + (nb + 1) * 512],
                                start=(dc == 0), stop=(dc == 15)))
                    P()
                    O("vector", lambda e: e.tensor_tensor(out=fA[:, 0:1024], in0=ps2a[:, 0:1024], in1=fA[:, 0:1024], op=ALU.add))
                    P()
                    DMA("gpsimd", h_d[r0:r0 + 128, :], fA[:, 0:1024])
                    P()
            P()
            for ec in range(8):
                DMA("sync", wblk_sb[:, ec * 1024:(ec + 1) * 1024], woutb_d[ec, :, :])
            DMA("sync", fG[0:1, 0:1024], bout_d.rearrange("(o j) -> o j", o=1))
            P()
            O("tensor", lambda e: e.matmul(ps2b[:, 0:512], ones1x128, fG[0:1, 0:512], start=True, stop=True))
            O("tensor", lambda e: e.matmul(ps2b[:, 512:1024], ones1x128, fG[0:1, 512:1024], start=True, stop=True))
            P()
            O("vector", lambda e: e.tensor_copy(out=dbca[:, 0:1024], in_=ps2b[:, 0:1024]))
            P()
            for tc in range(16):
                r0 = tok0 + tc * 128
                P()
                DMA("sync", fA[:, 0:1024], h_d[r0:r0 + 128, :])
                DMA("sync", fB[:, 0:1024], in_resid[r0:r0 + 128, :])
                P()
                O("vector", lambda e: e.tensor_copy(out=lwb[:, 0:1024], in_=fA[:, 0:1024]))
                P()
                transpose_8(lwb[:, 0:1024],
                            xn_T.rearrange("p (ec t) -> p ec t", ec=8)[:, :, tc * 128:tc * 128 + 128])
                for nb in range(2):
                    for ec in range(8):
                        O("tensor", lambda e, nb=nb, ec=ec, tc=tc: e.matmul(
                            ps2a[:, nb * 512:(nb + 1) * 512],
                            xn_T[:, ec * 2048 + tc * 128: ec * 2048 + tc * 128 + 128],
                            wblk_sb[:, ec * 1024 + nb * 512: ec * 1024 + (nb + 1) * 512],
                            start=(ec == 0), stop=(ec == 7)))
                P()
                O("vector", lambda e: e.tensor_tensor(out=fB[:, 0:1024], in0=ps2a[:, 0:1024], in1=fB[:, 0:1024], op=ALU.add))
                P()
                O("vector", lambda e: e.tensor_tensor(out=fB[:, 0:1024], in0=dbca[:, 0:1024], in1=fB[:, 0:1024], op=ALU.add))
                P()
                DMA("gpsimd", out_resid[r0:r0 + 128, :], fB[:, 0:1024])
                P()

    ffn_stage(x_d, r1_d)
    attn_stage(r1_d, r2_d)
    ffn_stage(r2_d, out_d)

    with es, nc.Block() as blk:
        total = bld.emit(blk, SEM)

        @blk.sync
        def _(sync):
            sync.wait_ge(SEM, total)

    return nc


def kernel(**inputs):
    gi = lambda k: np.ascontiguousarray(inputs[k], dtype=np.float32)
    x = gi("L0_attn_x").reshape(B_FULL * S, E)
    Wx = gi("L0_attn_Wx")
    Wx_re = np.zeros((L, D, 128), np.float32)
    Wx_re[:, :, 0:16] = Wx[:, :, 64:80]
    Wx_re[:, :, 32:48] = Wx[:, :, 80:96]
    Wx_re[:, :, 64:128] = Wx[:, :, 0:64]
    sel = np.zeros((16, 2048), np.float32)
    for n in range(16):
        sel[n, n * 128:(n + 1) * 128] = 1.0
    common = {
        "norm_w": gi("L0_attn_norm_w"),
        "Win": gi("L0_attn_Win"),
        "conv_w": gi("L0_attn_conv_w"),
        "conv_b": gi("L0_attn_conv_b"),
        "Wx_re": Wx_re,
        "Wdt": gi("L0_attn_Wdt"),
        "bdt": gi("L0_attn_bdt"),
        "A_log": gi("L0_attn_A_log"),
        "Dp": gi("L0_attn_Dp"),
        "Wblk": gi("L0_attn_Wblk"),
        "Wout": gi("L0_attn_Wout"),
        "bout": gi("L0_attn_bout"),
        "rW1T": np.ascontiguousarray(gi("L0_ffn_rW1").T),
        "rb1": gi("L0_ffn_rb1"),
        "ln_g": gi("L0_ffn_ln_g"),
        "ln_b": gi("L0_ffn_ln_b"),
        "rW2T": np.ascontiguousarray(gi("L0_ffn_rW2").T),
        "rb2": gi("L0_ffn_rb2"),
        "qWT": np.ascontiguousarray(gi("L0_ffn_qW").transpose(0, 2, 1)),
        "qb": gi("L0_ffn_qb"),
        "kWT": np.ascontiguousarray(gi("L0_ffn_kW").transpose(0, 2, 1)),
        "kb": gi("L0_ffn_kb"),
        "vWT": np.ascontiguousarray(gi("L0_ffn_vW").transpose(0, 2, 1)),
        "vb": gi("L0_ffn_vb"),
        "oWT": np.ascontiguousarray(gi("L0_ffn_oW").transpose(0, 2, 1)),
        "ob": gi("L0_ffn_ob"),
        "eye16": np.eye(128, dtype=ml_dtypes.bfloat16),
        "sel": sel,
    }
    nc = build_program()
    in_maps = []
    for c in range(NCORES):
        m = dict(common)
        m["x"] = np.ascontiguousarray(x[c * TOK:(c + 1) * TOK, :])
        in_maps.append(m)
    res = run_bass_kernel_spmd(nc, in_maps, core_ids=list(range(NCORES)))
    out = np.concatenate([r["out"] for r in res.results], axis=0)
    return out.reshape(B_FULL, S, E)
